# version 10
# baseline (speedup 1.0000x reference)
import contextlib
import numpy as np

import concourse.bass as bass
import concourse.mybir as mybir
from concourse.bass_utils import run_bass_kernel_spmd

F32 = mybir.dt.float32
BF16 = mybir.dt.bfloat16
AF = mybir.ActivationFunctionType
OP = mybir.AluOpType

P = 128
D = 1024
FF = 4096
T = 4096
TT = 512
NT = T // TT
KC = D // P
FC = FF // P
E = 4
EPS_LN = 1e-5
EPS_RMS = 1e-6
EPS_IN = 1e-12
RD = 1.0 / D

ENGINES = ("pe", "act", "dve", "sp", "gp")
QUEUES = {"qsp": 8, "qact": 6, "qgp": 6}
ISSUER = {"qsp": "sp", "qact": "act", "qgp": "gp"}


class _Op:
    __slots__ = ("eng", "fn", "queue", "deps", "sidx", "qidx", "inc", "tag")


class Sched:
    def __init__(self):
        self.ops = []
        self.streams = {e: [] for e in ENGINES}
        self.qcount = {q: 0 for q in QUEUES}
        self.last_w = {}
        self.readers = {}

    def add(self, eng, fn, reads=(), writes=(), queue=None, tag=""):
        op = _Op()
        op.eng = eng
        op.fn = fn
        op.queue = queue
        op.tag = tag
        deps = set()
        for r in reads:
            w = self.last_w.get(r)
            if w is not None:
                deps.add(w)
        for r in writes:
            w = self.last_w.get(r)
            if w is not None:
                deps.add(w)
            for rd in self.readers.get(r, {}).values():
                deps.add(rd)
        deps.discard(op)
        op.deps = deps
        op.sidx = len(self.streams[eng])
        self.streams[eng].append(op)
        op.qidx = None
        if queue is not None:
            op.qidx = self.qcount[queue]
            self.qcount[queue] += 1
        op.inc = None
        self.ops.append(op)
        rkey = eng if queue is None else ("dmaq", queue, op.qidx % QUEUES[queue])
        for r in reads:
            self.readers.setdefault(r, {})[rkey] = op
        for r in writes:
            self.last_w[r] = op
            self.readers[r] = {rkey: op}
        return op

    def dma(self, queue, fn, reads=(), writes=(), tag=""):
        return self.add(ISSUER[queue], fn, reads, writes, queue=queue, tag=tag)

    def finalize(self):
        need = set()
        for op in self.ops:
            for d in op.deps:
                if d.queue is not None:
                    continue
                if d.eng == op.eng and op.queue is None and d.eng == "pe":
                    continue
                need.add(d)
        self.final_counts = {}
        for e, stream in self.streams.items():
            c = 0
            for op in stream:
                if op.queue is None and op in need:
                    c += 1
                    op.inc = c
            self.final_counts[e] = c

    def emit_stream(self, ename, eng, esems, qsems):
        wm = {}

        def qwait(queue, qidx, extra=1):
            nsem = QUEUES[queue]
            sub, rank = qidx % nsem, qidx // nsem
            sem = qsems[queue][sub]
            val = 16 * (rank + extra)
            key = ("q", queue, sub)
            if wm.get(key, 0) >= val:
                return
            eng.wait_ge(sem, val)
            wm[key] = val

        for op in self.streams[ename]:
            for d in sorted(op.deps, key=lambda o: (o.eng, o.sidx)):
                if d.queue is not None:
                    if op.queue == d.queue and (op.qidx % QUEUES[op.queue]) == (
                            d.qidx % QUEUES[d.queue]):
                        continue
                    qwait(d.queue, d.qidx)
                    continue
                if d.eng == ename and op.queue is None and d.eng == "pe":
                    continue
                assert d.inc is not None, (d.tag, op.tag)
                sem, val, key = esems[d.eng], d.inc, ("e", d.eng)
                if wm.get(key, 0) >= val:
                    continue
                eng.wait_ge(sem, val)
                wm[key] = val
            if op.queue is not None:
                nsem = QUEUES[op.queue]
                if op.qidx >= nsem:
                    qwait(op.queue, op.qidx - nsem)
            inst = op.fn(eng)
            if op.queue is not None:
                inst.then_inc(qsems[op.queue][op.qidx % QUEUES[op.queue]], 16)
            elif op.inc is not None:
                inst.then_inc(esems[ename], 1)
        for q, issuer in ISSUER.items():
            if issuer == ename and self.qcount[q] > 0:
                for sub in range(QUEUES[q]):
                    n = (self.qcount[q] - sub + QUEUES[q] - 1) // QUEUES[q]
                    if n > 0:
                        eng.wait_ge(qsems[q][sub], 16 * n)


def _wt_blocks(out_feats, in_feats):
    n_mg2 = out_feats // 256
    n_kg = in_feats // 512
    return n_mg2, n_kg


def build_nc(debug=False):
    nc = bass.Bass("TRN2", target_bir_lowering=False, debug=False)

    di = {}
    def din(name, shape):
        di[name] = nc.dram_tensor(name, shape, F32, kind="ExternalInput").ap()
        return di[name]

    xt = din("xt", [D, T])
    wlnz = din("wlnz", [D, D]); wdt = din("wdt", [D, D])
    wy = din("wy", [D, D]); wyact = din("wyact", [D, D])
    wfc = din("wfc", [D, FF]); wfca = din("wfca", [D, FF])
    wout = din("wout", [FF, D])
    w1 = din("w1", [E, D, FF])
    wo = din("wo", [FF, D])
    wrt = din("wrt", [P, KC, E])
    b1 = din("b1", [E, FF])
    blnz_h = din("blnz", [P, KC]); bdt_h = din("bdt", [P, KC])
    by_h = din("by", [P, KC]); byact_h = din("byact", [P, KC])
    bout_h = din("bout", [P, KC]); bo_h = din("bo", [P, KC])
    gam_h = din("gam", [P, KC]); bet_h = din("bet", [P, KC])
    gsio_h = din("gsio", [P, KC]); gffn_h = din("gffn", [P, KC])
    h0_h = din("h0", [P, KC])
    bfc_h = din("bfc", [P, FC]); bfca_h = din("bfca", [P, FC])

    out_t = nc.dram_tensor("out", [D, T], F32, kind="ExternalOutput").ap()

    scr = {}
    for name, of, inf in [("wlnz", D, D), ("wdt", D, D), ("wy", D, D),
                          ("wyact", D, D), ("wfc", FF, D), ("wfca", FF, D),
                          ("wout", D, FF), ("wo", D, FF)]:
        n_mg2, n_kg = _wt_blocks(of, inf)
        scr[name] = nc.dram_tensor("scr_" + name, [n_mg2, n_kg, 4, P, 256],
                                   BF16, kind="Internal").ap()
    n_mg2, n_kg = _wt_blocks(FF, D)
    scr["w1"] = nc.dram_tensor("scr_w1", [E, n_mg2, n_kg, 4, P, 256],
                               BF16, kind="Internal").ap()

    sched = Sched()

    with contextlib.ExitStack() as stack:
        def sb(nm, shp, dt):
            return stack.enter_context(nc.sbuf_tensor(nm, shp, dt))

        ax0b = [sb("ax0a", [P, T], F32), sb("ax0c", [P, T], F32)]
        at1 = sb("at1", [P, T], F32)
        aal = sb("aal", [P, T], F32)
        au = sb("au", [P, T], F32)
        ba = sb("ba", [P, T], BF16)
        bbig1 = sb("bbig1", [P, 4 * T], BF16)
        bbig2 = sb("bbig2", [P, 4 * T], BF16)
        scv = bbig1[:].bitcast(F32)
        ws = [sb(f"ws{i}", [P, 1024], BF16) for i in range(8)]
        st_ab = [sb("st_a0", [P, TT], F32), sb("st_a1", [P, TT], F32)]
        st_bb = [sb("st_b0", [P, TT], F32), sb("st_b1", [P, TT], F32)]
        tmpb = [[sb("tmp00", [P, TT], F32), sb("tmp01", [P, TT], F32)],
                [sb("tmp10", [P, TT], F32), sb("tmp11", [P, TT], F32)]]
        mbb = [bbig2[:, i * TT:(i + 1) * TT] for i in range(E)]
        mbf = [bbig2[0:1, (4 + i) * TT:(5 + i) * TT] for i in range(E)]
        bcv = bbig2[:].bitcast(F32)
        upc = lambda m: scv[:, (KC + m) * TT:(KC + m + 1) * TT]
        alc = lambda m: bcv[:, (4 + m) * TT:(4 + m + 1) * TT]
        upregs = lambda m: [("bb1", 16 + 2 * m), ("bb1", 17 + 2 * m)]
        alregs = lambda m: [("bb2", 8 + 2 * m), ("bb2", 9 + 2 * m)]
        sc0 = sb("sc0", [P, TT], F32)
        sc1 = sb("sc1", [P, TT], F32)
        l4 = sb("l4", [4, TT], F32)
        lrows = sb("lrows", [1, TT], F32)
        pool = sb("pool", [P, 2304], F32)
        rwk = pool[0:1, :2048]
        b1all = pool[32:36, :2048].bitcast(BF16)
        mrows = pool[32:36, 2048:2304].bitcast(BF16)
        negsRD = sb("negsRD", [4, 1], F32)
        exb = sb("exb", [4, 4], F32)
        cst = sb("cst", [P, 8], F32)
        ones = sb("ones", [P, P], F32)
        obf = sb("obf", [1, P], BF16)
        wrt_s = sb("wrt_s", [P, KC * E], F32)
        blnz_s = sb("blnz_s", [P, KC], F32); bdt_s = sb("bdt_s", [P, KC], F32)
        nbdt_s = sb("nbdt_s", [P, KC], F32)
        by_s = sb("by_s", [P, KC], F32); byact_s = sb("byact_s", [P, KC], F32)
        bout_s = sb("bout_s", [P, KC], F32); bo_s = sb("bo_s", [P, KC], F32)
        gam_s = sb("gam_s", [P, KC], F32); bet_s = sb("bet_s", [P, KC], F32)
        gsio_s = sb("gsio_s", [P, KC], F32); gffn_s = sb("gffn_s", [P, KC], F32)
        gneg_s = sb("gneg_s", [P, KC], F32)
        h0_s = sb("h0_s", [P, KC], F32)
        he = sb("he", [P, KC], F32)
        bfc_s = sb("bfc_s", [P, FC], F32); bfca_s = sb("bfca_s", [P, FC], F32)

        pb = [stack.enter_context(nc.psum_tensor(f"pb{i}", [P, TT], F32))
              for i in range(8)]

        esems = {e: stack.enter_context(nc.semaphore("sem_" + e)) for e in ENGINES}
        qsems = {q: [stack.enter_context(nc.semaphore(f"sem_{q}_{i}"))
                     for i in range(n)] for q, n in QUEUES.items()}

        A = sched.add
        DMA = sched.dma

        dbg_outs = {}
        def tap(name, src_ap_fn, regions, shape, dt=F32):
            if not debug:
                return
            d = nc.dram_tensor("dbg_" + name, shape, dt, kind="ExternalOutput").ap()
            dbg_outs[name] = d
            DMA("qact", lambda e, dd=d: e.dma_start(dd, src_ap_fn()),
                reads=regions, tag="tap_" + name)

        def arena_regions(name, chunks=range(KC)):
            return [(name, k) for k in chunks]

        def c3(t):
            return t[:].rearrange("p (c t) -> p c t", c=KC)

        def chk(t, k):
            return t[:, k * TT:(k + 1) * TT]

        def rep(stile):
            return stile[:].unsqueeze(1).broadcast_to([P, KC, TT])

        bank_ctr = [0]
        def banks(n):
            out = [(bank_ctr[0] + i) % 8 for i in range(n)]
            bank_ctr[0] += n
            return out

        ws_ctr = [0]
        def ws_slot():
            s = ws_ctr[0] % 8
            ws_ctr[0] += 1
            return s

        SS = {"p": 0}
        def use_stats(pp):
            SS["p"] = pp
        def st_a_():
            return st_ab[SS["p"]]
        def st_b_():
            return st_bb[SS["p"]]
        def st_regs():
            return ("st_a%d" % SS["p"], "st_b%d" % SS["p"])

        tmp_ctr = [0, 0]
        def tmp_slot():
            p = SS["p"]
            s = tmp_ctr[p] % 2
            tmp_ctr[p] += 1
            return tmpb[p][s], ("tmp", p, s)

        for tens, hb, reg in [
            (blnz_s, blnz_h, "blnz"), (bdt_s, bdt_h, "bdt"), (by_s, by_h, "by"),
            (byact_s, byact_h, "byact"), (bout_s, bout_h, "bout"), (bo_s, bo_h, "bo"),
            (gam_s, gam_h, "gam"), (bet_s, bet_h, "bet"), (gsio_s, gsio_h, "gsio"),
            (gffn_s, gffn_h, "gffn"), (h0_s, h0_h, "h0"),
            (bfc_s, bfc_h, "bfc"), (bfca_s, bfca_h, "bfca"),
        ]:
            DMA("qgp", lambda e, d=tens, s=hb: e.dma_start(d[:], s),
                writes=[reg], tag="ld_" + reg)
        DMA("qgp", lambda e: e.dma_start(wrt_s[:].rearrange("p (c e) -> p c e", e=E), wrt),
            writes=["wrt"], tag="ld_wrt")
        DMA("qgp", lambda e: e.dma_start(b1all[:], b1), writes=["b1all"], tag="ld_b1")

        A("dve", lambda e: e.memset(ones[:], 1.0), writes=["ones"], tag="ones")
        A("dve", lambda e: e.memset(obf[:], 1.0), writes=["obf"], tag="obf")
        A("dve", lambda e: e.memset(exb[:], 0.0), writes=["exb0"], tag="exb0")
        CVAL = {EPS_LN: 0, EPS_RMS: 1, EPS_IN: 2, -1.0: 3, RD: 4,
                EPS_LN ** 0.5: 5, 1.0 + EPS_RMS: 6}
        for cv, ci in CVAL.items():
            A("dve", lambda e, i=ci, v=cv: e.memset(cst[:, i:i + 1], v),
              writes=["cst"], tag=f"cst{ci}")
        CP = lambda v: cst[:, CVAL[v]:CVAL[v] + 1]
        for j in range(1, 4):
            DMA("qact", lambda e, jj=j: e.dma_start(exb[jj:jj + 1, jj:jj + 1],
                                                    ones[0:1, 0:1]),
                reads=["ones", "exb0"], writes=["exb0"], tag=f"exb{j}")
        A("dve", lambda e: e.tensor_scalar(nbdt_s[:], bdt_s[:], -1.0, None, op0=OP.mult),
          reads=["bdt"], writes=["nbdt"], tag="nbdt")
        A("dve", lambda e: e.tensor_scalar(gneg_s[:], gsio_s[:], -1.0, None, op0=OP.mult),
          reads=["gsio"], writes=["gneg"], tag="gneg")
        A("dve", lambda e: e.tensor_copy(he[:], h0_s[:]),
          reads=["h0"], writes=[("he", k) for k in range(KC)], tag="he0")
        sb_bank = banks(1)[0]
        for k in range(KC):
            A("pe", lambda e, kk=k, st=(k == 0), sp=(k == KC - 1):
                e.matmul(pb[sb_bank][0:4, 0:1],
                         wrt_s[:, kk * E:(kk + 1) * E], ones[:, 0:1],
                         start=st, stop=sp),
              reads=["wrt", "ones"], writes=[("pb", sb_bank)], tag=f"se{k}")
        A("dve", lambda e: e.tensor_scalar(negsRD[:], pb[sb_bank][0:4, 0:1],
                                           -RD, None, op0=OP.mult),
          reads=[("pb", sb_bank)], writes=["negsRD"], tag="negsRD")

        v32 = {}
        for name, src_ap in [("wlnz", wlnz), ("wdt", wdt), ("wy", wy),
                             ("wyact", wyact), ("wfc", wfc), ("wfca", wfca),
                             ("wout", wout), ("wo", wo)]:
            v32[name] = src_ap.rearrange("(kg kk p) (mg c) -> mg kg kk p c",
                                         p=P, kk=4, c=256)
        v32["w1"] = w1.rearrange("e (kg kk p) (mg c) -> e mg kg kk p c",
                                 p=P, kk=4, c=256)
        CUR = {"tile": 0}

        def load_block(name, mg2, kg, extra_key=None):
            s = ws_slot()
            reg = ("ws", s)
            direct = CUR["tile"] == 0
            key = (name, mg2, kg) if extra_key is None else (name, extra_key, mg2, kg)
            scr_ap = (scr[name][mg2, kg] if extra_key is None
                      else scr[name][extra_key, mg2, kg])
            dview = ws[s][:, :4 * 256].rearrange("p (kk c) -> p kk c", kk=4)
            if direct:
                src = (v32[name][mg2, kg] if extra_key is None
                       else v32[name][extra_key, mg2, kg]).rearrange("kk p c -> p kk c")
                DMA("qgp", lambda e, dv=dview, sv=src: e.dma_start(dv, sv),
                    reads=[], writes=[reg], tag=f"ld0_{name}_{mg2}_{kg}")
                DMA("qsp", lambda e, dv=dview, sa=scr_ap: e.dma_start(
                        sa.rearrange("kk p c -> p kk c"), dv),
                    reads=[reg], writes=[("scr",) + key], tag=f"st0_{name}_{mg2}_{kg}")
            else:
                src = scr_ap.rearrange("kk p c -> p kk c")
                DMA("qsp", lambda e, dv=dview, sv=src: e.dma_start(dv, sv),
                    reads=[("scr",) + key], writes=[reg], tag=f"ld_{name}_{mg2}_{kg}")
            return s, reg

        def lhsT(slot, k_in_group, j):
            off = k_in_group * 256 + j * P
            return ws[slot][:, off:off + P]

        def stats(src, src_name, need_mean, eps, tag, fixed=None):
            if fixed is not None:
                b2, b1_ = fixed
            else:
                b2 = banks(1)[0]
                b1_ = banks(1)[0] if need_mean else None
            st_a, st_b = st_a_(), st_b_()
            ra, rb_r = st_regs()
            for k in range(KC):
                tp, treg = tmp_slot()
                A("act", lambda e, s=chk(src, k), d=tp: e.activation(d[:], s, AF.Square),
                  reads=[(src_name, k)], writes=[treg], tag=f"{tag}_sq{k}")
                A("pe", lambda e, d=pb[b2], s=tp, st=(k == 0), sp=(k == KC - 1):
                    e.matmul(d[:], ones[:], s[:], start=st, stop=sp),
                  reads=[treg, "ones"], writes=[("pb", b2)], tag=f"{tag}_mm2_{k}")
                if need_mean:
                    A("pe", lambda e, d=pb[b1_], s=chk(src, k), st=(k == 0), sp=(k == KC - 1):
                        e.matmul(d[:], ones[:], s, start=st, stop=sp),
                      reads=[(src_name, k), "ones"], writes=[("pb", b1_)], tag=f"{tag}_mm1_{k}")
            if need_mean:
                tp, treg = tmp_slot()
                A("act", lambda e, d=tp, s=pb[b1_]: e.activation(
                    d[:], s[:], AF.Square, scale=CP(RD)),
                  reads=[("pb", b1_), "cst"], writes=[treg], tag=f"{tag}_mu2")
                tp2, treg2 = tmp_slot()
                A("dve", lambda e, s=pb[b2], t=tp, d=tp2: e.scalar_tensor_tensor(
                    d[:], s[:], RD, t[:], op0=OP.mult, op1=OP.subtract),
                  reads=[("pb", b2), treg], writes=[treg2], tag=f"{tag}_var")
                tp3, treg3 = tmp_slot()
                A("act", lambda e, t=tp2, d=tp3, ep=eps: e.activation(
                    d[:], t[:], AF.Sqrt, bias=CP(ep)),
                  reads=[treg2, "cst"], writes=[treg3], tag=f"{tag}_sd")
                A("dve", lambda e, t=tp3, d=st_a: e.reciprocal(d[:], t[:]),
                  reads=[treg3], writes=[ra], tag=f"{tag}_rb")
                A("dve", lambda e, s=pb[b1_], a=st_a, d=st_b: e.scalar_tensor_tensor(
                    d[:], s[:], RD, a[:], op0=OP.mult, op1=OP.mult),
                  reads=[("pb", b1_), ra], writes=[rb_r], tag=f"{tag}_tb")
            else:
                tp3, treg3 = tmp_slot()
                A("act", lambda e, s=pb[b2], d=tp3, ep=eps: e.activation(
                    d[:], s[:], AF.Sqrt, bias=CP(ep), scale=CP(RD)),
                  reads=[("pb", b2), "cst"], writes=[treg3], tag=f"{tag}_sd")
                A("dve", lambda e, t=tp3, d=st_a: e.reciprocal(d[:], t[:]),
                  reads=[treg3], writes=[ra], tag=f"{tag}_rb")
            return b2, b1_

        def rms_from_ln(tag):
            st_a = st_a_()
            ra = st_regs()[0]
            tp, treg = tmp_slot()
            A("act", lambda e, d=tp, a=st_a: e.activation(
                d[:], a[:], AF.Square, scale=CP(EPS_LN ** 0.5)),
              reads=[ra, "cst"], writes=[treg], tag=f"{tag}_q")
            tp2, treg2 = tmp_slot()
            A("act", lambda e, s=tp, d=tp2: e.activation(
                d[:], s[:], AF.Sqrt, scale=CP(-1.0), bias=CP(1.0 + EPS_RMS)),
              reads=[treg, "cst"], writes=[treg2], tag=f"{tag}_sd")
            A("dve", lambda e, t=tp2: e.reciprocal(sc1[:], t[:]),
              reads=[treg2], writes=["sc1"], tag=f"{tag}_rb")

        def ln_apply(src, src_name, dst, dst_name, tag, resid=None, resid_name=None):
            st_a, st_b = st_a_(), st_b_()
            ra, rb_r = st_regs()
            A("dve", lambda e, a=st_a: e.tensor_tensor(c3(dst), c3(src), rep(a), op=OP.mult),
              reads=arena_regions(src_name) + [ra],
              writes=arena_regions(dst_name), tag=f"{tag}_ap1")
            A("dve", lambda e, b=st_b: e.tensor_tensor(c3(dst), c3(dst), rep(b), op=OP.subtract),
              reads=arena_regions(dst_name) + [rb_r],
              writes=arena_regions(dst_name), tag=f"{tag}_ap2")
            if resid is not None:
                A("dve", lambda e: e.tensor_tensor(c3(resid), c3(resid), c3(dst), op=OP.add),
                  reads=arena_regions(resid_name) + arena_regions(dst_name),
                  writes=arena_regions(resid_name), tag=f"{tag}_res")

        def rms_apply(src, src_name, gtile, greg, tag, use_b=False):
            if use_b:
                rbt, rbreg = st_b_(), st_regs()[1]
            else:
                rbt, rbreg = st_a_(), st_regs()[0]
            for k in range(KC):
                A("dve", lambda e, kk=k, g=gtile, r=rbt: e.scalar_tensor_tensor(
                    chk(ba, kk), chk(src, kk), g[:, kk:kk + 1], r[:],
                    op0=OP.mult, op1=OP.mult),
                  reads=[(src_name, k), greg, rbreg], writes=[("ba", k)],
                  tag=f"{tag}_ap{k}")

        def mm_phase(wname, n_out_chunks, n_k, rhs_reg_fn, rhs_ap_fn, evac_fn,
                     tag, extra_key=None, extra_weight=None):
            n_half = n_out_chunks // 2
            n_kg = n_k // 4
            for h in range(n_half):
                nb = 4 if extra_weight else 2
                bk = banks(nb)
                slots = []
                for kg in range(n_kg):
                    slots.append(load_block(wname, h, kg, extra_key))
                    if extra_weight:
                        slots.append(load_block(extra_weight[0], h, kg))
                for k in range(n_k):
                    kg, kk = divmod(k, 4)
                    if extra_weight:
                        s1, r1 = slots[2 * kg]
                        s2, r2 = slots[2 * kg + 1]
                    else:
                        s1, r1 = slots[kg]
                    rreg = rhs_reg_fn(k)
                    rap = rhs_ap_fn(k)
                    for j in range(2):
                        A("pe", lambda e, b=pb[bk[j]], w=lhsT(s1, kk, j), r=rap,
                            st=(k == 0), sp=(k == n_k - 1):
                            e.matmul(b[:], w, r, start=st, stop=sp),
                          reads=[r1, rreg], writes=[("pb", bk[j])],
                          tag=f"{tag}_h{h}k{k}j{j}")
                        if extra_weight:
                            A("pe", lambda e, b=pb[bk[2 + j]], w=lhsT(s2, kk, j), r=rap,
                                st=(k == 0), sp=(k == n_k - 1):
                                e.matmul(b[:], w, r, start=st, stop=sp),
                              reads=[r2, rreg], writes=[("pb", bk[2 + j])],
                              tag=f"{tag}x_h{h}k{k}j{j}")
                for j in range(2):
                    m = 2 * h + j
                    evac_fn(bk[j], m, (bk[2 + j] if extra_weight else None))

        def x0_load(tj):
            tslj = slice(tj * TT, (tj + 1) * TT)
            srcj = xt.rearrange("(k p) t -> p k t", p=P)[:, :, tslj]
            DMA("qact", lambda e, s=srcj, d=ax0b[tj % 2]: e.dma_start(c3(d), s),
                writes=arena_regions(f"x0{tj % 2}"), tag=f"t{tj}_x0")

        ZB = [2, 3, 4, 5]
        LN1B = (0, 1)
        LNIB = (6, 7)
        LN4B = (6, 7)

        def ln_apply_chunked(src, src_name, dst, dst_name, tag):
            st_a, st_b = st_a_(), st_b_()
            ra, rb_r = st_regs()
            for k in range(KC):
                A("dve", lambda e, kk=k, a=st_a: e.tensor_tensor(
                    chk(dst, kk), chk(src, kk), a[:], op=OP.mult),
                  reads=[(src_name, k), ra], writes=[(dst_name, k)],
                  tag=f"{tag}_ap1{k}")
                A("dve", lambda e, kk=k, b=st_b: e.tensor_tensor(
                    chk(dst, kk), chk(dst, kk), b[:], op=OP.subtract),
                  reads=[(dst_name, k), rb_r], writes=[(dst_name, k)],
                  tag=f"{tag}_ap2{k}")

        def zdt_mms(tj, h, hs):
            for k in range(KC):
                kg, kk = divmod(k, 4)
                s1, r1 = hs[2 * kg]
                s2, r2 = hs[2 * kg + 1]
                for j in range(2):
                    A("pe", lambda e, b=pb[ZB[j]], w=lhsT(s1, kk, j), r=chk(ba, k),
                        st=(k == 0), sp=(k == KC - 1):
                        e.matmul(b[:], w, r, start=st, stop=sp),
                      reads=[r1, ("ba", k)], writes=[("pb", ZB[j])],
                      tag=f"t{tj}_zdt_h{h}k{k}j{j}")
                    A("pe", lambda e, b=pb[ZB[2 + j]], w=lhsT(s2, kk, j), r=chk(ba, k),
                        st=(k == 0), sp=(k == KC - 1):
                        e.matmul(b[:], w, r, start=st, stop=sp),
                      reads=[r2, ("ba", k)], writes=[("pb", ZB[2 + j])],
                      tag=f"t{tj}_zdtx_h{h}k{k}j{j}")

        def zdt_evac(tj, bz, m, bd):
            A("act", lambda e, b=pb[bz], mm=m: e.activation(
                upc(mm), b[:], AF.Sigmoid, bias=blnz_s[:, mm:mm + 1]),
              reads=[("pb", bz), "blnz"], writes=upregs(m), tag=f"t{tj}_s1_{m}")
            A("act", lambda e, b=pb[bd], mm=m: e.activation(
                alc(mm), b[:], AF.Sigmoid, bias=nbdt_s[:, mm:mm + 1],
                scale=CP(-1.0)),
              reads=[("pb", bd), "nbdt", "cst"], writes=alregs(m), tag=f"t{tj}_al_{m}")
            tp, treg = tmp_slot()
            A("act", lambda e, b=pb[bd], mm=m, d=tp: e.activation(
                d[:], b[:], AF.Sigmoid, bias=bdt_s[:, mm:mm + 1]),
              reads=[("pb", bd), "bdt"], writes=[treg], tag=f"t{tj}_s2_{m}")
            A("dve", lambda e, mm=m, s=tp: e.tensor_tensor(
                upc(mm), upc(mm), s[:], op=OP.mult),
              reads=upregs(m) + [treg], writes=upregs(m), tag=f"t{tj}_u_{m}")

        def zdt_loads(h):
            hs = []
            for kg in range(2):
                hs.append(load_block("wlnz", h, kg))
                hs.append(load_block("wdt", h, kg))
            return [hs[0], hs[1], hs[2], hs[3]]

        def emit_front_a(tj):
            CUR["tile"] = tj
            use_stats(tj % 2)
            axj = ax0b[tj % 2]
            xnj = f"x0{tj % 2}"
            stats(axj, xnj, True, EPS_LN, f"t{tj}_ln1", fixed=LN1B)
            rms_from_ln(f"t{tj}_rms1")
            ra, rb_r = st_regs()
            A("dve", lambda e, a=st_a_(): e.tensor_tensor(
                sc0[:], a[:], sc1[:], op=OP.mult),
              reads=[ra, "sc1"], writes=["sc0"], tag=f"t{tj}_sv")
            A("dve", lambda e, b=st_b_(): e.tensor_tensor(
                sc1[:], b[:], sc1[:], op=OP.mult),
              reads=[rb_r, "sc1"], writes=["sc1"], tag=f"t{tj}_ov")
            for k in range(KC):
                tp, treg = tmp_slot()
                A("dve", lambda e, kk=k, d=tp: e.scalar_tensor_tensor(
                    d[:], chk(axj, kk), gneg_s[:, kk:kk + 1], sc0[:],
                    op0=OP.mult, op1=OP.mult),
                  reads=[(xnj, k), "gneg", "sc0"], writes=[treg],
                  tag=f"t{tj}_bat{k}")
                A("dve", lambda e, kk=k, s=tp: e.scalar_tensor_tensor(
                    chk(ba, kk), sc1[:], gneg_s[:, kk:kk + 1], s[:],
                    op0=OP.mult, op1=OP.subtract),
                  reads=["sc1", "gneg", treg], writes=[("ba", k)],
                  tag=f"t{tj}_ba{k}")
            hs0 = zdt_loads(0)
            hs1 = zdt_loads(1)
            zdt_mms(tj, 0, hs0)
            for k in range(KC):
                A("gp", lambda e, kk=k, a=st_a_(): e.tensor_tensor(
                    chk(at1, kk), chk(axj, kk), a[:], op=OP.mult),
                  reads=[(xnj, k), ra], writes=[("t1", k)], tag=f"t{tj}_t1a{k}")
                A("gp", lambda e, kk=k, b=st_b_(): e.tensor_tensor(
                    chk(at1, kk), chk(at1, kk), b[:], op=OP.subtract),
                  reads=[("t1", k), rb_r], writes=[("t1", k)], tag=f"t{tj}_t1b{k}")
            return hs1

        def emit_front_b(tj, hs1):
            CUR["tile"] = tj
            use_stats(tj % 2)
            if 2 <= tj + 1 < NT:
                x0_load(tj + 1)
            for j in range(2):
                zdt_evac(tj, ZB[j], j, ZB[2 + j])
            zdt_mms(tj, 1, hs1)
            for j in range(2):
                zdt_evac(tj, ZB[j], 2 + j, ZB[2 + j])
            for h in (2, 3):
                hs = zdt_loads(h)
                zdt_mms(tj, h, hs)
                for j in range(2):
                    zdt_evac(tj, ZB[j], 2 * h + j, ZB[2 + j])
            for k in range(KC):
                screg = [("bb1", 2 * k), ("bb1", 2 * k + 1)]
                A("dve", lambda e, kk=k: e.tensor_tensor_scan(
                    scv[:, kk * TT:(kk + 1) * TT], alc(kk), upc(kk),
                    he[:, kk:kk + 1], op0=OP.mult, op1=OP.add),
                  reads=alregs(k) + upregs(k) + [("he", k)], writes=screg,
                  tag=f"t{tj}_scan{k}")
                A("dve", lambda e, kk=k: e.tensor_copy(
                    bbig2[:, kk * TT:(kk + 1) * TT], scv[:, kk * TT:(kk + 1) * TT]),
                  reads=screg, writes=[("bb2", k)], tag=f"t{tj}_hb{k}")
                A("dve", lambda e, kk=k: e.tensor_copy(
                    he[:, kk:kk + 1], scv[:, kk * TT + TT - 1:kk * TT + TT]),
                  reads=screg, writes=[("he", k)], tag=f"t{tj}_hecp{k}")

        def emit_mid(ti):
            CUR["tile"] = ti
            use_stats(ti % 2)
            ax = ax0b[ti % 2]
            xnm = f"x0{ti % 2}"

            def yact_evac(bk_, m, _):
                A("act", lambda e, b=pb[bk_], mm=m: e.activation(
                    chk(au, mm), b[:], AF.Silu, bias=byact_s[:, mm:mm + 1]),
                  reads=[("pb", bk_), "byact"], writes=[("u", m)], tag=f"t{ti}_sy_{m}")

            mm_phase("wyact", KC, KC, lambda k: ("ba", k),
                     lambda k: chk(ba, k), yact_evac, f"t{ti}_yact")

            def y_evac(bk_, m, _):
                tp, treg = tmp_slot()
                A("dve", lambda e, b=pb[bk_], mm=m, d=tp: e.scalar_tensor_tensor(
                    d[:], b[:], by_s[:, mm:mm + 1], chk(au, mm), op0=OP.add, op1=OP.mult),
                  reads=[("pb", bk_), "by", ("u", m)], writes=[treg], tag=f"t{ti}_y_{m}")
                A("dve", lambda e, mm=m, s=tp: e.tensor_tensor(
                    chk(at1, mm), chk(at1, mm), s[:], op=OP.add),
                  reads=[("t1", m), treg], writes=[("t1", m)], tag=f"t{ti}_xp_{m}")

            mm_phase("wy", KC, KC, lambda k: ("bb2", k),
                     lambda k: bbig2[:, k * TT:(k + 1) * TT], y_evac, f"t{ti}_y")
            if ti == 0:
                tap("xp", lambda: at1[:], arena_regions("t1"), [P, T])

            stats(at1, "t1", False, EPS_RMS, f"t{ti}_rms2")
            rms_apply(at1, "t1", gffn_s, "gffn", f"t{ti}_rms2")

            def fc_evac(bfc_bank, m, bfca_bank):
                tp, treg = tmp_slot()
                A("act", lambda e, b=pb[bfca_bank], mm=m, d=tp: e.activation(
                    d[:], b[:], AF.Silu, bias=bfca_s[:, mm:mm + 1]),
                  reads=[("pb", bfca_bank), "bfca"], writes=[treg], tag=f"t{ti}_fs_{m}")
                A("dve", lambda e, b=pb[bfc_bank], mm=m, s=tp: e.scalar_tensor_tensor(
                    bbig1[:, mm * TT:(mm + 1) * TT], b[:], bfc_s[:, mm:mm + 1], s[:],
                    op0=OP.add, op1=OP.mult),
                  reads=[("pb", bfc_bank), "bfc", treg], writes=[("bb1", m)],
                  tag=f"t{ti}_hid_{m}")

            mm_phase("wfc", FC, KC, lambda k: ("ba", k),
                     lambda k: chk(ba, k), fc_evac, f"t{ti}_fc",
                     extra_weight=("wfca", None))
            if ti == 0:
                tap("hid", lambda: bbig1[:], [("bb1", c) for c in range(FC)],
                    [P, 4 * T], BF16)

            def out_evac(bk_, m, _):
                A("dve", lambda e, b=pb[bk_], mm=m: e.scalar_tensor_tensor(
                    chk(au, mm), b[:], bout_s[:, mm:mm + 1], chk(at1, mm),
                    op0=OP.add, op1=OP.add),
                  reads=[("pb", bk_), "bout", ("t1", m)], writes=[("u", m)],
                  tag=f"t{ti}_ao_{m}")

            mm_phase("wout", KC, FC, lambda k: ("bb1", k),
                     lambda k: bbig1[:, k * TT:(k + 1) * TT], out_evac, f"t{ti}_out")

            if ti == 0:
                tap("ao", lambda: au[:], arena_regions("u"), [P, T])

            stats(au, "u", True, EPS_LN, f"t{ti}_ln2")
            ln_apply(au, "u", au, "u", f"t{ti}_ln2", resid=ax, resid_name=xnm)

            rb_ = banks(1)[0]
            for k in range(KC):
                A("pe", lambda e, b=pb[rb_], kk=k, axx=ax, st=(k == 0), sp=(k == KC - 1):
                    e.matmul(b[0:4, :],
                             wrt_s[:, kk * E:(kk + 1) * E], chk(axx, kk),
                             start=st, stop=sp),
                  reads=["wrt", (xnm, k)], writes=[("pb", rb_)], tag=f"t{ti}_rt{k}")
            _, b1_ln3 = stats(ax, xnm, True, EPS_LN, f"t{ti}_ln3")
            A("dve", lambda e, b=pb[rb_]: e.tensor_copy(l4[:], b[0:4, :]),
              reads=[("pb", rb_)], writes=["l4"], tag=f"t{ti}_l4")
            if ti == 0:
                tap("x1t", lambda axx=ax: axx[:], arena_regions(xnm), [P, T])
                tap("l4raw", lambda: l4[:], ["l4"], [4, TT])
            A("dve", lambda e, s=pb[b1_ln3]: e.scalar_tensor_tensor(
                l4[:], s[0:4, :], negsRD[:, 0:1], l4[:], op0=OP.mult, op1=OP.add),
              reads=[("pb", b1_ln3), "negsRD", "l4"], writes=["l4"],
              tag=f"t{ti}_lsh")
            bx = banks(3)
            for idx in range(3):
                A("pe", lambda e, b=pb[bx[idx]], jj=idx + 1: e.matmul(
                    b[0:1, :], exb[0:4, jj:jj + 1], l4[0:4, :],
                    start=True, stop=True),
                  reads=["exb0", "l4"], writes=[("pb", bx[idx])],
                  tag=f"t{ti}_ex{idx}")
            A("dve", lambda e, b=pb[bx[1]]: e.tensor_copy(lrows[:], b[0:1, :]),
              reads=[("pb", bx[1])], writes=[("lr", 0)], tag=f"t{ti}_l2")
            R = lambda i: rwk[:, i * TT:(i + 1) * TT]
            A("dve", lambda e, b=pb[bx[0]]: e.tensor_tensor(
                R(0), l4[0:1, :], b[0:1, :], op=OP.max),
              reads=["l4", ("pb", bx[0])], writes=[("rw", 0)], tag=f"t{ti}_mx1")
            A("dve", lambda e, b=pb[bx[2]]: e.tensor_tensor(
                R(3), lrows[:], b[0:1, :], op=OP.max),
              reads=[("lr", 0), ("pb", bx[2])], writes=[("rw", 3)], tag=f"t{ti}_mx2")
            A("dve", lambda e: e.tensor_tensor(R(0), R(0), R(3), op=OP.max),
              reads=[("rw", 0), ("rw", 3)], writes=[("rw", 0)], tag=f"t{ti}_mx3")

            mbk = banks(4)
            for ee_ in range(E):
                if ee_ == 0:
                    A("dve", lambda e: e.tensor_tensor(R(1), l4[0:1, :], R(0), op=OP.is_equal),
                      reads=["l4", ("rw", 0)], writes=[("rw", 1)], tag=f"t{ti}_m0")
                    A("dve", lambda e: e.tensor_scalar(R(2), R(1), -1.0, 1.0,
                                                       op0=OP.mult, op1=OP.add),
                      reads=[("rw", 1)], writes=[("rw", 2)], tag=f"t{ti}_nm0")
                elif ee_ < 3:
                    lsrc = ((lambda e, b=pb[bx[0]]: e.tensor_tensor(
                                R(3), b[0:1, :], R(0), op=OP.is_equal))
                            if ee_ == 1 else
                            (lambda e: e.tensor_tensor(
                                R(3), lrows[:], R(0), op=OP.is_equal)))
                    lreg = ("pb", bx[0]) if ee_ == 1 else ("lr", 0)
                    A("dve", lsrc, reads=[lreg, ("rw", 0)], writes=[("rw", 3)],
                      tag=f"t{ti}_eq{ee_}")
                    A("dve", lambda e: e.tensor_tensor(R(1), R(3), R(2), op=OP.mult),
                      reads=[("rw", 3), ("rw", 2)], writes=[("rw", 1)], tag=f"t{ti}_m{ee_}")
                    A("dve", lambda e: e.tensor_scalar(R(3), R(1), -1.0, 1.0,
                                                       op0=OP.mult, op1=OP.add),
                      reads=[("rw", 1)], writes=[("rw", 3)], tag=f"t{ti}_t{ee_}")
                    A("dve", lambda e: e.tensor_tensor(R(2), R(2), R(3), op=OP.mult),
                      reads=[("rw", 2), ("rw", 3)], writes=[("rw", 2)], tag=f"t{ti}_nm{ee_}")
                else:
                    A("dve", lambda e: e.tensor_copy(R(1), R(2)),
                      reads=[("rw", 2)], writes=[("rw", 1)], tag=f"t{ti}_m3")
                A("dve", lambda e, ex=ee_: e.tensor_copy(mbf[ex], R(1)),
                  reads=[("rw", 1)], writes=[("bb2", 4 + ee_)], tag=f"t{ti}_mb{ee_}")
                DMA("qact", lambda e, ex=ee_: e.dma_start(mrows[ex:ex + 1, :], mbf[ex]),
                    reads=[("bb2", 4 + ee_)], writes=[("mr", ee_)], tag=f"t{ti}_mrd{ee_}")
                A("pe", lambda e, ex=ee_, b=pb[mbk[ee_]]: e.matmul(
                    b[:], obf[:], mbf[ex], start=True, stop=True),
                  reads=[("bb2", 4 + ee_), "obf"], writes=[("pb", mbk[ee_])],
                  tag=f"t{ti}_mbc{ee_}")

            ln_apply(ax, xnm, aal, "al", f"t{ti}_ln3")
            if ti == 0:
                tap("t2", lambda: aal[:], arena_regions("al"), [P, T])
            for ee_ in range(E):
                A("dve", lambda e, ex=ee_, b=pb[mbk[ee_]]: e.tensor_copy(
                    mbb[ex], b[:]),
                  reads=[("pb", mbk[ee_])], writes=[("bb2", ee_)], tag=f"t{ti}_mbe{ee_}")

            for ee_ in range(E):
                A("dve", lambda e, ex=ee_: e.tensor_tensor(
                    bbig1[:, ex * T:(ex + 1) * T].rearrange("p (c t) -> p c t", c=KC),
                    aal[:].rearrange("p (c t) -> p c t", c=KC),
                    mbb[ex].unsqueeze(1).broadcast_to([P, KC, TT]), op=OP.mult),
                  reads=arena_regions("al") + [("bb2", ee_)],
                  writes=[("bb1", ee_ * KC + k) for k in range(KC)], tag=f"t{ti}_xm{ee_}")

            if ti == 0:
                tap("l4", lambda: l4[:], ["l4"], [4, TT])
                tap("mr", lambda: mrows[:], [("mr", x) for x in range(E)], [4, TT], BF16)
                tap("xm", lambda: bbig1[:], [("bb1", c) for c in range(FC)],
                    [P, 4 * T], BF16)

            n_half1 = FC // 2
            for h in range(n_half1):
                bk = banks(2)
                slots = []
                for ee_ in range(E):
                    for kg in range(2):
                        slots.append(load_block("w1", h, kg, extra_key=ee_))
                for ee_ in range(E):
                    for k in range(KC):
                        kg, kk = divmod(k, 4)
                        s1, r1 = slots[ee_ * 2 + kg]
                        for j in range(2):
                            A("pe", lambda e, b=pb[bk[j]], w=lhsT(s1, kk, j),
                                r=bbig1[:, ee_ * T + k * TT: ee_ * T + (k + 1) * TT],
                                st=(ee_ == 0 and k == 0), sp=False:
                                e.matmul(b[:], w, r, start=st, stop=sp),
                              reads=[r1, ("bb1", ee_ * KC + k)], writes=[("pb", bk[j])],
                              tag=f"t{ti}_w1_h{h}e{ee_}k{k}j{j}")
                for j in range(2):
                    c = 2 * h + j
                    A("pe", lambda e, b=pb[bk[j]], cc=c: e.matmul(
                        b[:], b1all[:, cc * P:(cc + 1) * P], mrows[:],
                        start=False, stop=True),
                      reads=["b1all"] + [("mr", x) for x in range(E)],
                      writes=[("pb", bk[j])], tag=f"t{ti}_w1b_h{h}j{j}")
                for j in range(2):
                    c = 2 * h + j
                    A("act", lambda e, b=pb[bk[j]], cc=c: e.activation(
                        bbig2[:, cc * TT:(cc + 1) * TT], b[:], AF.Gelu),
                      reads=[("pb", bk[j])], writes=[("bb2", c)], tag=f"t{ti}_gel{c}")

            def wo_evac(bk_, m, _):
                A("dve", lambda e, b=pb[bk_], mm=m: e.scalar_tensor_tensor(
                    chk(aal, mm), b[:], bo_s[:, mm:mm + 1], chk(aal, mm),
                    op0=OP.add, op1=OP.add),
                  reads=[("pb", bk_), "bo", ("al", m)], writes=[("al", m)],
                  tag=f"t{ti}_yf_{m}")

            mm_phase("wo", KC, FC, lambda k: ("bb2", k),
                     lambda k: bbig2[:, k * TT:(k + 1) * TT], wo_evac, f"t{ti}_wo")

        def emit_tail(ti):
            CUR["tile"] = ti
            use_stats(ti % 2)
            ax = ax0b[ti % 2]
            xnm = f"x0{ti % 2}"
            stats(aal, "al", True, EPS_IN, f"t{ti}_lni", fixed=LNIB)
            ln_apply(aal, "al", aal, "al", f"t{ti}_lni")
            for k in range(KC):
                A("act", lambda e, kk=k: e.activation(
                    chk(au, kk), chk(aal, kk), AF.Identity,
                    bias=bet_s[:, kk:kk + 1], scale=gam_s[:, kk:kk + 1]),
                  reads=[("al", k), "gam", "bet"], writes=[("u", k)],
                  tag=f"t{ti}_gb{k}")

            stats(au, "u", True, EPS_LN, f"t{ti}_ln4", fixed=LN4B)
            ln_apply(au, "u", au, "u", f"t{ti}_ln4", resid=ax, resid_name=xnm)

            dst = out_t.rearrange("(k p) t -> p k t", p=P)[
                :, :, ti * TT:(ti + 1) * TT]
            DMA("qact", lambda e, d=dst, axx=ax: e.dma_start(d, c3(axx)),
                reads=arena_regions(xnm), tag=f"t{ti}_st")

        x0_load(0)
        x0_load(1)
        hs1 = emit_front_a(0)
        emit_front_b(0, hs1)
        for ti in range(NT):
            emit_mid(ti)
            hs1 = emit_front_a(ti + 1) if ti + 1 < NT else None
            emit_tail(ti)
            if hs1 is not None:
                emit_front_b(ti + 1, hs1)

        nc._sbuf_rem = nc.sbuf_bytes_remaining
        sched.finalize()
        blk = stack.enter_context(nc.Block())

        @blk.tensor
        def _(eng):
            sched.emit_stream("pe", eng, esems, qsems)

        @blk.scalar
        def _(eng):
            sched.emit_stream("act", eng, esems, qsems)

        @blk.vector
        def _(eng):
            sched.emit_stream("dve", eng, esems, qsems)

        @blk.sync
        def _(eng):
            sched.emit_stream("sp", eng, esems, qsems)

        @blk.gpsimd
        def _(eng):
            sched.emit_stream("gp", eng, esems, qsems)

    nc._sched = sched
    return nc


def _host_layout(inputs):
    x = np.asarray(inputs["L0_attn_x"])
    def wT(a):
        return np.ascontiguousarray(np.asarray(a).T)
    def vchunk(v, n):
        return np.ascontiguousarray(np.asarray(v).reshape(n, P).T)

    common = {
        "wlnz": wT(inputs["L0_attn_w_ln_z"]),
        "wdt": wT(inputs["L0_attn_w_dt"]),
        "wy": wT(inputs["L0_attn_w_y"]),
        "wyact": wT(inputs["L0_attn_w_yact"]),
        "wfc": wT(inputs["L0_attn_w_fc"]),
        "wfca": wT(inputs["L0_attn_w_fcact"]),
        "wout": wT(inputs["L0_attn_w_out"]),
        "w1": np.ascontiguousarray(np.asarray(inputs["L0_ffn_W1"])),
        "wo": np.ascontiguousarray(np.asarray(inputs["L0_ffn_Wo"])),
        "wrt": np.ascontiguousarray(
            np.asarray(inputs["L0_ffn_W_router"]).reshape(KC, P, E).transpose(1, 0, 2)),
        "b1": np.ascontiguousarray(np.asarray(inputs["L0_ffn_b1"])),
        "blnz": vchunk(inputs["L0_attn_b_ln_z"], KC),
        "bdt": vchunk(inputs["L0_attn_b_dt"], KC),
        "by": vchunk(inputs["L0_attn_b_y"], KC),
        "byact": vchunk(inputs["L0_attn_b_yact"], KC),
        "bout": vchunk(inputs["L0_attn_b_out"], KC),
        "bo": vchunk(inputs["L0_ffn_bo"], KC),
        "gam": vchunk(inputs["L0_ffn_ln_gamma"], KC),
        "bet": vchunk(inputs["L0_ffn_ln_beta"], KC),
        "gsio": vchunk(inputs["L0_attn_g_sio"], KC),
        "gffn": vchunk(inputs["L0_attn_g_ffn"], KC),
        "h0": vchunk(inputs["L0_attn_h0"], KC),
        "bfc": vchunk(inputs["L0_attn_b_fc"], FC),
        "bfca": vchunk(inputs["L0_attn_b_fcact"], FC),
    }
    in_maps = []
    for b in range(8):
        m = dict(common)
        m["xt"] = np.ascontiguousarray(x[b].T)
        in_maps.append(m)
    return in_maps


def kernel(**inputs):
    in_maps = _host_layout(inputs)
    nc = build_nc()
    res = run_bass_kernel_spmd(nc, in_maps, core_ids=list(range(8)))
    outs = []
    for b in range(8):
        o = np.asarray(res.results[b]["out"], dtype=np.float32)
        outs.append(o.T)
    return np.ascontiguousarray(np.stack(outs, axis=0))


# version 11
# speedup vs baseline: 1.0579x; 1.0579x over previous
import contextlib
import numpy as np

import concourse.bass as bass
import concourse.mybir as mybir
from concourse.bass_utils import run_bass_kernel_spmd

F32 = mybir.dt.float32
BF16 = mybir.dt.bfloat16
AF = mybir.ActivationFunctionType
OP = mybir.AluOpType

P = 128
D = 1024
FF = 4096
T = 4096
TT = 512
NT = T // TT
KC = D // P
FC = FF // P
E = 4
EPS_LN = 1e-5
EPS_RMS = 1e-6
EPS_IN = 1e-12
RD = 1.0 / D

ENGINES = ("pe", "act", "dve", "sp", "gp")
QUEUES = {"qsp": 8, "qact": 6, "qgp": 6}
ISSUER = {"qsp": "sp", "qact": "act", "qgp": "gp"}


class _Op:
    __slots__ = ("eng", "fn", "queue", "deps", "sidx", "qidx", "inc", "tag")


class Sched:
    def __init__(self):
        self.ops = []
        self.streams = {e: [] for e in ENGINES}
        self.qcount = {q: 0 for q in QUEUES}
        self.last_w = {}
        self.readers = {}

    def add(self, eng, fn, reads=(), writes=(), queue=None, tag=""):
        op = _Op()
        op.eng = eng
        op.fn = fn
        op.queue = queue
        op.tag = tag
        deps = set()
        for r in reads:
            w = self.last_w.get(r)
            if w is not None:
                deps.add(w)
        for r in writes:
            w = self.last_w.get(r)
            if w is not None:
                deps.add(w)
            for rd in self.readers.get(r, {}).values():
                deps.add(rd)
        deps.discard(op)
        op.deps = deps
        op.sidx = len(self.streams[eng])
        self.streams[eng].append(op)
        op.qidx = None
        if queue is not None:
            op.qidx = self.qcount[queue]
            self.qcount[queue] += 1
        op.inc = None
        self.ops.append(op)
        rkey = eng if queue is None else ("dmaq", queue, op.qidx % QUEUES[queue])
        for r in reads:
            self.readers.setdefault(r, {})[rkey] = op
        for r in writes:
            self.last_w[r] = op
            self.readers[r] = {rkey: op}
        return op

    def dma(self, queue, fn, reads=(), writes=(), tag=""):
        return self.add(ISSUER[queue], fn, reads, writes, queue=queue, tag=tag)

    def finalize(self):
        need = set()
        for op in self.ops:
            for d in op.deps:
                if d.queue is not None:
                    continue
                if d.eng == op.eng and op.queue is None and d.eng == "pe":
                    continue
                need.add(d)
        self.final_counts = {}
        for e, stream in self.streams.items():
            c = 0
            for op in stream:
                if op.queue is None and op in need:
                    c += 1
                    op.inc = c
            self.final_counts[e] = c

    def emit_stream(self, ename, eng, esems, qsems):
        wm = {}

        def qwait(queue, qidx, extra=1):
            nsem = QUEUES[queue]
            sub, rank = qidx % nsem, qidx // nsem
            sem = qsems[queue][sub]
            val = 16 * (rank + extra)
            key = ("q", queue, sub)
            if wm.get(key, 0) >= val:
                return
            eng.wait_ge(sem, val)
            wm[key] = val

        for op in self.streams[ename]:
            for d in sorted(op.deps, key=lambda o: (o.eng, o.sidx)):
                if d.queue is not None:
                    if op.queue == d.queue and (op.qidx % QUEUES[op.queue]) == (
                            d.qidx % QUEUES[d.queue]):
                        continue
                    qwait(d.queue, d.qidx)
                    continue
                if d.eng == ename and op.queue is None and d.eng == "pe":
                    continue
                assert d.inc is not None, (d.tag, op.tag)
                sem, val, key = esems[d.eng], d.inc, ("e", d.eng)
                if wm.get(key, 0) >= val:
                    continue
                eng.wait_ge(sem, val)
                wm[key] = val
            if op.queue is not None:
                nsem = QUEUES[op.queue]
                if op.qidx >= nsem:
                    qwait(op.queue, op.qidx - nsem)
            inst = op.fn(eng)
            if op.queue is not None:
                inst.then_inc(qsems[op.queue][op.qidx % QUEUES[op.queue]], 16)
            elif op.inc is not None:
                inst.then_inc(esems[ename], 1)
        for q, issuer in ISSUER.items():
            if issuer == ename and self.qcount[q] > 0:
                for sub in range(QUEUES[q]):
                    n = (self.qcount[q] - sub + QUEUES[q] - 1) // QUEUES[q]
                    if n > 0:
                        eng.wait_ge(qsems[q][sub], 16 * n)


def _wt_blocks(out_feats, in_feats):
    n_mg2 = out_feats // 256
    n_kg = in_feats // 512
    return n_mg2, n_kg


def build_nc(debug=False):
    nc = bass.Bass("TRN2", target_bir_lowering=False, debug=False)

    di = {}
    def din(name, shape):
        di[name] = nc.dram_tensor(name, shape, F32, kind="ExternalInput").ap()
        return di[name]

    xt = din("xt", [D, T])
    wlnz = din("wlnz", [D, D]); wdt = din("wdt", [D, D])
    wy = din("wy", [D, D]); wyact = din("wyact", [D, D])
    wfc = din("wfc", [D, FF]); wfca = din("wfca", [D, FF])
    wout = din("wout", [FF, D])
    w1 = din("w1", [E, D, FF])
    wo = din("wo", [FF, D])
    wrt = din("wrt", [P, KC, E])
    b1 = din("b1", [E, FF])
    blnz_h = din("blnz", [P, KC]); bdt_h = din("bdt", [P, KC])
    by_h = din("by", [P, KC]); byact_h = din("byact", [P, KC])
    bout_h = din("bout", [P, KC]); bo_h = din("bo", [P, KC])
    gam_h = din("gam", [P, KC]); bet_h = din("bet", [P, KC])
    gsio_h = din("gsio", [P, KC]); gffn_h = din("gffn", [P, KC])
    h0_h = din("h0", [P, KC])
    bfc_h = din("bfc", [P, FC]); bfca_h = din("bfca", [P, FC])

    out_t = nc.dram_tensor("out", [D, T], F32, kind="ExternalOutput").ap()

    scr = {}
    for name, of, inf in [("wlnz", D, D), ("wdt", D, D), ("wy", D, D),
                          ("wyact", D, D), ("wfc", FF, D), ("wfca", FF, D),
                          ("wout", D, FF), ("wo", D, FF)]:
        n_mg2, n_kg = _wt_blocks(of, inf)
        scr[name] = nc.dram_tensor("scr_" + name, [n_mg2, n_kg, 4, P, 256],
                                   BF16, kind="Internal").ap()
    n_mg2, n_kg = _wt_blocks(FF, D)
    scr["w1"] = nc.dram_tensor("scr_w1", [E, n_mg2, n_kg, 4, P, 256],
                               BF16, kind="Internal").ap()

    sched = Sched()

    with contextlib.ExitStack() as stack:
        def sb(nm, shp, dt):
            return stack.enter_context(nc.sbuf_tensor(nm, shp, dt))

        ax0b = [sb("ax0a", [P, T], F32), sb("ax0c", [P, T], F32)]
        at1 = sb("at1", [P, T], F32)
        aal = sb("aal", [P, T], F32)
        au = sb("au", [P, T], F32)
        ba = sb("ba", [P, T], BF16)
        bbig1 = sb("bbig1", [P, 4 * T], BF16)
        bbig2 = sb("bbig2", [P, 4 * T], BF16)
        scv = bbig1[:].bitcast(F32)
        ws = [sb(f"ws{i}", [P, 1024], BF16) for i in range(8)]
        st_ab = [sb("st_a0", [P, TT], F32), sb("st_a1", [P, TT], F32)]
        st_bb = [sb("st_b0", [P, TT], F32), sb("st_b1", [P, TT], F32)]
        tmpb = [[sb("tmp00", [P, TT], F32), sb("tmp01", [P, TT], F32)],
                [sb("tmp10", [P, TT], F32), sb("tmp11", [P, TT], F32)]]
        mbb = [bbig2[:, i * TT:(i + 1) * TT] for i in range(E)]
        mbf = [bbig2[0:1, (4 + i) * TT:(5 + i) * TT] for i in range(E)]
        bcv = bbig2[:].bitcast(F32)
        upc = lambda m: scv[:, (KC + m) * TT:(KC + m + 1) * TT]
        alc = lambda m: bcv[:, (4 + m) * TT:(4 + m + 1) * TT]
        upregs = lambda m: [("bb1", 16 + 2 * m), ("bb1", 17 + 2 * m)]
        alregs = lambda m: [("bb2", 8 + 2 * m), ("bb2", 9 + 2 * m)]
        sc0 = sb("sc0", [P, TT], F32)
        sc1 = sb("sc1", [P, TT], F32)
        l4 = sb("l4", [4, TT], F32)
        lrows = sb("lrows", [1, TT], F32)
        pool = sb("pool", [P, 2304], F32)
        rwk = pool[0:1, :2048]
        b1all = pool[32:36, :2048].bitcast(BF16)
        mrows = pool[32:36, 2048:2304].bitcast(BF16)
        negsRD = sb("negsRD", [4, 1], F32)
        exb = sb("exb", [4, 4], F32)
        cst = sb("cst", [P, 8], F32)
        ones = sb("ones", [P, P], F32)
        obf = sb("obf", [1, P], BF16)
        wrt_s = sb("wrt_s", [P, KC * E], F32)
        blnz_s = sb("blnz_s", [P, KC], F32); bdt_s = sb("bdt_s", [P, KC], F32)
        nbdt_s = sb("nbdt_s", [P, KC], F32)
        by_s = sb("by_s", [P, KC], F32); byact_s = sb("byact_s", [P, KC], F32)
        bout_s = sb("bout_s", [P, KC], F32); bo_s = sb("bo_s", [P, KC], F32)
        gam_s = sb("gam_s", [P, KC], F32); bet_s = sb("bet_s", [P, KC], F32)
        gsio_s = sb("gsio_s", [P, KC], F32); gffn_s = sb("gffn_s", [P, KC], F32)
        gneg_s = sb("gneg_s", [P, KC], F32)
        h0_s = sb("h0_s", [P, KC], F32)
        he = sb("he", [P, KC], F32)
        bfc_s = sb("bfc_s", [P, FC], F32); bfca_s = sb("bfca_s", [P, FC], F32)

        pb = [stack.enter_context(nc.psum_tensor(f"pb{i}", [P, TT], F32))
              for i in range(8)]

        esems = {e: stack.enter_context(nc.semaphore("sem_" + e)) for e in ENGINES}
        qsems = {q: [stack.enter_context(nc.semaphore(f"sem_{q}_{i}"))
                     for i in range(n)] for q, n in QUEUES.items()}

        A = sched.add
        DMA = sched.dma

        dbg_outs = {}
        def tap(name, src_ap_fn, regions, shape, dt=F32):
            if not debug:
                return
            d = nc.dram_tensor("dbg_" + name, shape, dt, kind="ExternalOutput").ap()
            dbg_outs[name] = d
            DMA("qact", lambda e, dd=d: e.dma_start(dd, src_ap_fn()),
                reads=regions, tag="tap_" + name)

        def arena_regions(name, chunks=range(KC)):
            return [(name, k) for k in chunks]

        def c3(t):
            return t[:].rearrange("p (c t) -> p c t", c=KC)

        def chk(t, k):
            return t[:, k * TT:(k + 1) * TT]

        def rep(stile):
            return stile[:].unsqueeze(1).broadcast_to([P, KC, TT])

        bank_ctr = [0]
        def banks(n):
            out = [(bank_ctr[0] + i) % 8 for i in range(n)]
            bank_ctr[0] += n
            return out

        ws_ctr = [0]
        def ws_slot():
            s = ws_ctr[0] % 8
            ws_ctr[0] += 1
            return s

        SS = {"p": 0}
        def use_stats(pp):
            SS["p"] = pp
        def st_a_():
            return st_ab[SS["p"]]
        def st_b_():
            return st_bb[SS["p"]]
        def st_regs():
            return ("st_a%d" % SS["p"], "st_b%d" % SS["p"])

        tmp_ctr = [0, 0]
        def tmp_slot():
            p = SS["p"]
            s = tmp_ctr[p] % 2
            tmp_ctr[p] += 1
            return tmpb[p][s], ("tmp", p, s)

        for tens, hb, reg in [
            (blnz_s, blnz_h, "blnz"), (bdt_s, bdt_h, "bdt"), (by_s, by_h, "by"),
            (byact_s, byact_h, "byact"), (bout_s, bout_h, "bout"), (bo_s, bo_h, "bo"),
            (gam_s, gam_h, "gam"), (bet_s, bet_h, "bet"), (gsio_s, gsio_h, "gsio"),
            (gffn_s, gffn_h, "gffn"), (h0_s, h0_h, "h0"),
            (bfc_s, bfc_h, "bfc"), (bfca_s, bfca_h, "bfca"),
        ]:
            DMA("qgp", lambda e, d=tens, s=hb: e.dma_start(d[:], s),
                writes=[reg], tag="ld_" + reg)
        DMA("qgp", lambda e: e.dma_start(wrt_s[:].rearrange("p (c e) -> p c e", e=E), wrt),
            writes=["wrt"], tag="ld_wrt")
        DMA("qgp", lambda e: e.dma_start(b1all[:], b1), writes=["b1all"], tag="ld_b1")

        A("dve", lambda e: e.memset(ones[:], 1.0), writes=["ones"], tag="ones")
        A("dve", lambda e: e.memset(obf[:], 1.0), writes=["obf"], tag="obf")
        A("dve", lambda e: e.memset(exb[:], 0.0), writes=["exb0"], tag="exb0")
        CVAL = {EPS_LN: 0, EPS_RMS: 1, EPS_IN: 2, -1.0: 3, RD: 4,
                EPS_LN ** 0.5: 5, 1.0 + EPS_RMS: 6}
        for cv, ci in CVAL.items():
            A("dve", lambda e, i=ci, v=cv: e.memset(cst[:, i:i + 1], v),
              writes=["cst"], tag=f"cst{ci}")
        CP = lambda v: cst[:, CVAL[v]:CVAL[v] + 1]
        for j in range(1, 4):
            DMA("qact", lambda e, jj=j: e.dma_start(exb[jj:jj + 1, jj:jj + 1],
                                                    ones[0:1, 0:1]),
                reads=["ones", "exb0"], writes=["exb0"], tag=f"exb{j}")
        A("dve", lambda e: e.tensor_scalar(nbdt_s[:], bdt_s[:], -1.0, None, op0=OP.mult),
          reads=["bdt"], writes=["nbdt"], tag="nbdt")
        A("dve", lambda e: e.tensor_scalar(gneg_s[:], gsio_s[:], -1.0, None, op0=OP.mult),
          reads=["gsio"], writes=["gneg"], tag="gneg")
        A("dve", lambda e: e.tensor_copy(he[:], h0_s[:]),
          reads=["h0"], writes=[("he", k) for k in range(KC)], tag="he0")
        sb_bank = banks(1)[0]
        for k in range(KC):
            A("pe", lambda e, kk=k, st=(k == 0), sp=(k == KC - 1):
                e.matmul(pb[sb_bank][0:4, 0:1],
                         wrt_s[:, kk * E:(kk + 1) * E], ones[:, 0:1],
                         start=st, stop=sp),
              reads=["wrt", "ones"], writes=[("pb", sb_bank)], tag=f"se{k}")
        A("dve", lambda e: e.tensor_scalar(negsRD[:], pb[sb_bank][0:4, 0:1],
                                           -RD, None, op0=OP.mult),
          reads=[("pb", sb_bank)], writes=["negsRD"], tag="negsRD")

        v32 = {}
        for name, src_ap in [("wlnz", wlnz), ("wdt", wdt), ("wy", wy),
                             ("wyact", wyact), ("wfc", wfc), ("wfca", wfca),
                             ("wout", wout), ("wo", wo)]:
            v32[name] = src_ap.rearrange("(kg kk p) (mg c) -> mg kg kk p c",
                                         p=P, kk=4, c=256)
        v32["w1"] = w1.rearrange("e (kg kk p) (mg c) -> e mg kg kk p c",
                                 p=P, kk=4, c=256)
        CUR = {"tile": 0}

        def load_block(name, mg2, kg, extra_key=None):
            s = ws_slot()
            reg = ("ws", s)
            direct = CUR["tile"] == 0
            key = (name, mg2, kg) if extra_key is None else (name, extra_key, mg2, kg)
            scr_ap = (scr[name][mg2, kg] if extra_key is None
                      else scr[name][extra_key, mg2, kg])
            dview = ws[s][:, :4 * 256].rearrange("p (kk c) -> p kk c", kk=4)
            if direct:
                src = (v32[name][mg2, kg] if extra_key is None
                       else v32[name][extra_key, mg2, kg]).rearrange("kk p c -> p kk c")
                DMA("qgp", lambda e, dv=dview, sv=src: e.dma_start(dv, sv),
                    reads=[], writes=[reg], tag=f"ld0_{name}_{mg2}_{kg}")
                DMA("qsp", lambda e, dv=dview, sa=scr_ap: e.dma_start(
                        sa.rearrange("kk p c -> p kk c"), dv),
                    reads=[reg], writes=[("scr",) + key], tag=f"st0_{name}_{mg2}_{kg}")
            else:
                src = scr_ap.rearrange("kk p c -> p kk c")
                DMA("qsp", lambda e, dv=dview, sv=src: e.dma_start(dv, sv),
                    reads=[("scr",) + key], writes=[reg], tag=f"ld_{name}_{mg2}_{kg}")
            return s, reg

        def lhsT(slot, k_in_group, j):
            off = k_in_group * 256 + j * P
            return ws[slot][:, off:off + P]

        def stats(src, src_name, need_mean, eps, tag, fixed=None):
            if fixed is not None:
                b2, b1_ = fixed
            else:
                b2 = banks(1)[0]
                b1_ = banks(1)[0] if need_mean else None
            st_a, st_b = st_a_(), st_b_()
            ra, rb_r = st_regs()
            for k in range(KC):
                tp, treg = tmp_slot()
                A("act", lambda e, s=chk(src, k), d=tp: e.activation(d[:], s, AF.Square),
                  reads=[(src_name, k)], writes=[treg], tag=f"{tag}_sq{k}")
                A("pe", lambda e, d=pb[b2], s=tp, st=(k == 0), sp=(k == KC - 1):
                    e.matmul(d[:], ones[:], s[:], start=st, stop=sp),
                  reads=[treg, "ones"], writes=[("pb", b2)], tag=f"{tag}_mm2_{k}")
                if need_mean:
                    A("pe", lambda e, d=pb[b1_], s=chk(src, k), st=(k == 0), sp=(k == KC - 1):
                        e.matmul(d[:], ones[:], s, start=st, stop=sp),
                      reads=[(src_name, k), "ones"], writes=[("pb", b1_)], tag=f"{tag}_mm1_{k}")
            if need_mean:
                tp, treg = tmp_slot()
                A("act", lambda e, d=tp, s=pb[b1_]: e.activation(
                    d[:], s[:], AF.Square, scale=CP(RD)),
                  reads=[("pb", b1_), "cst"], writes=[treg], tag=f"{tag}_mu2")
                tp2, treg2 = tmp_slot()
                A("dve", lambda e, s=pb[b2], t=tp, d=tp2: e.scalar_tensor_tensor(
                    d[:], s[:], RD, t[:], op0=OP.mult, op1=OP.subtract),
                  reads=[("pb", b2), treg], writes=[treg2], tag=f"{tag}_var")
                tp3, treg3 = tmp_slot()
                A("act", lambda e, t=tp2, d=tp3, ep=eps: e.activation(
                    d[:], t[:], AF.Sqrt, bias=CP(ep)),
                  reads=[treg2, "cst"], writes=[treg3], tag=f"{tag}_sd")
                A("dve", lambda e, t=tp3, d=st_a: e.reciprocal(d[:], t[:]),
                  reads=[treg3], writes=[ra], tag=f"{tag}_rb")
                A("dve", lambda e, s=pb[b1_], a=st_a, d=st_b: e.scalar_tensor_tensor(
                    d[:], s[:], RD, a[:], op0=OP.mult, op1=OP.mult),
                  reads=[("pb", b1_), ra], writes=[rb_r], tag=f"{tag}_tb")
            else:
                tp3, treg3 = tmp_slot()
                A("act", lambda e, s=pb[b2], d=tp3, ep=eps: e.activation(
                    d[:], s[:], AF.Sqrt, bias=CP(ep), scale=CP(RD)),
                  reads=[("pb", b2), "cst"], writes=[treg3], tag=f"{tag}_sd")
                A("dve", lambda e, t=tp3, d=st_a: e.reciprocal(d[:], t[:]),
                  reads=[treg3], writes=[ra], tag=f"{tag}_rb")
            return b2, b1_

        def rms_from_ln(tag):
            st_a = st_a_()
            ra = st_regs()[0]
            tp, treg = tmp_slot()
            A("act", lambda e, d=tp, a=st_a: e.activation(
                d[:], a[:], AF.Square, scale=CP(EPS_LN ** 0.5)),
              reads=[ra, "cst"], writes=[treg], tag=f"{tag}_q")
            tp2, treg2 = tmp_slot()
            A("act", lambda e, s=tp, d=tp2: e.activation(
                d[:], s[:], AF.Sqrt, scale=CP(-1.0), bias=CP(1.0 + EPS_RMS)),
              reads=[treg, "cst"], writes=[treg2], tag=f"{tag}_sd")
            A("dve", lambda e, t=tp2: e.reciprocal(sc1[:], t[:]),
              reads=[treg2], writes=["sc1"], tag=f"{tag}_rb")

        def ln_apply(src, src_name, dst, dst_name, tag, resid=None, resid_name=None):
            st_a, st_b = st_a_(), st_b_()
            ra, rb_r = st_regs()
            A("dve", lambda e, a=st_a: e.tensor_tensor(c3(dst), c3(src), rep(a), op=OP.mult),
              reads=arena_regions(src_name) + [ra],
              writes=arena_regions(dst_name), tag=f"{tag}_ap1")
            A("dve", lambda e, b=st_b: e.tensor_tensor(c3(dst), c3(dst), rep(b), op=OP.subtract),
              reads=arena_regions(dst_name) + [rb_r],
              writes=arena_regions(dst_name), tag=f"{tag}_ap2")
            if resid is not None:
                A("dve", lambda e: e.tensor_tensor(c3(resid), c3(resid), c3(dst), op=OP.add),
                  reads=arena_regions(resid_name) + arena_regions(dst_name),
                  writes=arena_regions(resid_name), tag=f"{tag}_res")

        def rms_apply(src, src_name, gtile, greg, tag, use_b=False):
            if use_b:
                rbt, rbreg = st_b_(), st_regs()[1]
            else:
                rbt, rbreg = st_a_(), st_regs()[0]
            for k in range(KC):
                A("dve", lambda e, kk=k, g=gtile, r=rbt: e.scalar_tensor_tensor(
                    chk(ba, kk), chk(src, kk), g[:, kk:kk + 1], r[:],
                    op0=OP.mult, op1=OP.mult),
                  reads=[(src_name, k), greg, rbreg], writes=[("ba", k)],
                  tag=f"{tag}_ap{k}")

        def mm_phase(wname, n_out_chunks, n_k, rhs_reg_fn, rhs_ap_fn, evac_fn,
                     tag, extra_key=None, extra_weight=None):
            n_half = n_out_chunks // 2
            n_kg = n_k // 4
            for h in range(n_half):
                nb = 4 if extra_weight else 2
                bk = banks(nb)
                slots = []
                for kg in range(n_kg):
                    slots.append(load_block(wname, h, kg, extra_key))
                    if extra_weight:
                        slots.append(load_block(extra_weight[0], h, kg))
                for k in range(n_k):
                    kg, kk = divmod(k, 4)
                    if extra_weight:
                        s1, r1 = slots[2 * kg]
                        s2, r2 = slots[2 * kg + 1]
                    else:
                        s1, r1 = slots[kg]
                    rreg = rhs_reg_fn(k)
                    rap = rhs_ap_fn(k)
                    for j in range(2):
                        A("pe", lambda e, b=pb[bk[j]], w=lhsT(s1, kk, j), r=rap,
                            st=(k == 0), sp=(k == n_k - 1):
                            e.matmul(b[:], w, r, start=st, stop=sp),
                          reads=[r1, rreg], writes=[("pb", bk[j])],
                          tag=f"{tag}_h{h}k{k}j{j}")
                        if extra_weight:
                            A("pe", lambda e, b=pb[bk[2 + j]], w=lhsT(s2, kk, j), r=rap,
                                st=(k == 0), sp=(k == n_k - 1):
                                e.matmul(b[:], w, r, start=st, stop=sp),
                              reads=[r2, rreg], writes=[("pb", bk[2 + j])],
                              tag=f"{tag}x_h{h}k{k}j{j}")
                for j in range(2):
                    m = 2 * h + j
                    evac_fn(bk[j], m, (bk[2 + j] if extra_weight else None))

        def x0_load(tj):
            tslj = slice(tj * TT, (tj + 1) * TT)
            srcj = xt.rearrange("(k p) t -> p k t", p=P)[:, :, tslj]
            DMA("qact", lambda e, s=srcj, d=ax0b[tj % 2]: e.dma_start(c3(d), s),
                writes=arena_regions(f"x0{tj % 2}"), tag=f"t{tj}_x0")

        ZB = [2, 3, 4, 5]
        LN1B = (0, 1)
        LNIB = (6, 7)
        LN4B = (6, 7)

        def ln_apply_chunked(src, src_name, dst, dst_name, tag):
            st_a, st_b = st_a_(), st_b_()
            ra, rb_r = st_regs()
            for k in range(KC):
                A("dve", lambda e, kk=k, a=st_a: e.tensor_tensor(
                    chk(dst, kk), chk(src, kk), a[:], op=OP.mult),
                  reads=[(src_name, k), ra], writes=[(dst_name, k)],
                  tag=f"{tag}_ap1{k}")
                A("dve", lambda e, kk=k, b=st_b: e.tensor_tensor(
                    chk(dst, kk), chk(dst, kk), b[:], op=OP.subtract),
                  reads=[(dst_name, k), rb_r], writes=[(dst_name, k)],
                  tag=f"{tag}_ap2{k}")

        def zdt_mms(tj, h, hs):
            for k in range(KC):
                kg, kk = divmod(k, 4)
                s1, r1 = hs[2 * kg]
                s2, r2 = hs[2 * kg + 1]
                for j in range(2):
                    A("pe", lambda e, b=pb[ZB[j]], w=lhsT(s1, kk, j), r=chk(ba, k),
                        st=(k == 0), sp=(k == KC - 1):
                        e.matmul(b[:], w, r, start=st, stop=sp),
                      reads=[r1, ("ba", k)], writes=[("pb", ZB[j])],
                      tag=f"t{tj}_zdt_h{h}k{k}j{j}")
                    A("pe", lambda e, b=pb[ZB[2 + j]], w=lhsT(s2, kk, j), r=chk(ba, k),
                        st=(k == 0), sp=(k == KC - 1):
                        e.matmul(b[:], w, r, start=st, stop=sp),
                      reads=[r2, ("ba", k)], writes=[("pb", ZB[2 + j])],
                      tag=f"t{tj}_zdtx_h{h}k{k}j{j}")

        def zdt_evac(tj, bz, m, bd):
            A("act", lambda e, b=pb[bz], mm=m: e.activation(
                upc(mm), b[:], AF.Sigmoid, bias=blnz_s[:, mm:mm + 1]),
              reads=[("pb", bz), "blnz"], writes=upregs(m), tag=f"t{tj}_s1_{m}")
            A("act", lambda e, b=pb[bd], mm=m: e.activation(
                alc(mm), b[:], AF.Sigmoid, bias=nbdt_s[:, mm:mm + 1],
                scale=CP(-1.0)),
              reads=[("pb", bd), "nbdt", "cst"], writes=alregs(m), tag=f"t{tj}_al_{m}")
            tp, treg = tmp_slot()
            A("act", lambda e, b=pb[bd], mm=m, d=tp: e.activation(
                d[:], b[:], AF.Sigmoid, bias=bdt_s[:, mm:mm + 1]),
              reads=[("pb", bd), "bdt"], writes=[treg], tag=f"t{tj}_s2_{m}")
            A("dve", lambda e, mm=m, s=tp: e.tensor_tensor(
                upc(mm), upc(mm), s[:], op=OP.mult),
              reads=upregs(m) + [treg], writes=upregs(m), tag=f"t{tj}_u_{m}")

        def zdt_loads(h):
            hs = []
            for kg in range(2):
                hs.append(load_block("wlnz", h, kg))
                hs.append(load_block("wdt", h, kg))
            return [hs[0], hs[1], hs[2], hs[3]]

        def emit_front_a1(tj):
            CUR["tile"] = tj
            use_stats(tj % 2)
            axj = ax0b[tj % 2]
            xnj = f"x0{tj % 2}"
            stats(axj, xnj, True, EPS_LN, f"t{tj}_ln1", fixed=LN1B)
            rms_from_ln(f"t{tj}_rms1")
            ra, rb_r = st_regs()
            A("dve", lambda e, a=st_a_(): e.tensor_tensor(
                sc0[:], a[:], sc1[:], op=OP.mult),
              reads=[ra, "sc1"], writes=["sc0"], tag=f"t{tj}_sv")
            A("dve", lambda e, b=st_b_(): e.tensor_tensor(
                sc1[:], b[:], sc1[:], op=OP.mult),
              reads=[rb_r, "sc1"], writes=["sc1"], tag=f"t{tj}_ov")
            for k in range(KC):
                tp, treg = tmp_slot()
                A("dve", lambda e, kk=k, d=tp: e.scalar_tensor_tensor(
                    d[:], chk(axj, kk), gneg_s[:, kk:kk + 1], sc0[:],
                    op0=OP.mult, op1=OP.mult),
                  reads=[(xnj, k), "gneg", "sc0"], writes=[treg],
                  tag=f"t{tj}_bat{k}")
                A("dve", lambda e, kk=k, s=tp: e.scalar_tensor_tensor(
                    chk(ba, kk), sc1[:], gneg_s[:, kk:kk + 1], s[:],
                    op0=OP.mult, op1=OP.subtract),
                  reads=["sc1", "gneg", treg], writes=[("ba", k)],
                  tag=f"t{tj}_ba{k}")

        def emit_front_a2(tj):
            CUR["tile"] = tj
            use_stats(tj % 2)
            axj = ax0b[tj % 2]
            xnj = f"x0{tj % 2}"
            ra, rb_r = st_regs()
            hs0 = zdt_loads(0)
            hs1 = zdt_loads(1)
            zdt_mms(tj, 0, hs0)
            for k in range(KC):
                A("gp", lambda e, kk=k, a=st_a_(): e.tensor_tensor(
                    chk(at1, kk), chk(axj, kk), a[:], op=OP.mult),
                  reads=[(xnj, k), ra], writes=[("t1", k)], tag=f"t{tj}_t1a{k}")
                A("gp", lambda e, kk=k, b=st_b_(): e.tensor_tensor(
                    chk(at1, kk), chk(at1, kk), b[:], op=OP.subtract),
                  reads=[("t1", k), rb_r], writes=[("t1", k)], tag=f"t{tj}_t1b{k}")
            return hs1

        def emit_front_b(tj, hs1):
            CUR["tile"] = tj
            use_stats(tj % 2)
            if 2 <= tj + 1 < NT:
                x0_load(tj + 1)
            for j in range(2):
                zdt_evac(tj, ZB[j], j, ZB[2 + j])
            zdt_mms(tj, 1, hs1)
            for j in range(2):
                zdt_evac(tj, ZB[j], 2 + j, ZB[2 + j])
            for h in (2, 3):
                hs = zdt_loads(h)
                zdt_mms(tj, h, hs)
                for j in range(2):
                    zdt_evac(tj, ZB[j], 2 * h + j, ZB[2 + j])
            for k in range(KC):
                screg = [("bb1", 2 * k), ("bb1", 2 * k + 1)]
                A("dve", lambda e, kk=k: e.tensor_tensor_scan(
                    scv[:, kk * TT:(kk + 1) * TT], alc(kk), upc(kk),
                    he[:, kk:kk + 1], op0=OP.mult, op1=OP.add),
                  reads=alregs(k) + upregs(k) + [("he", k)], writes=screg,
                  tag=f"t{tj}_scan{k}")
                A("dve", lambda e, kk=k: e.tensor_copy(
                    bbig2[:, kk * TT:(kk + 1) * TT], scv[:, kk * TT:(kk + 1) * TT]),
                  reads=screg, writes=[("bb2", k)], tag=f"t{tj}_hb{k}")
                A("dve", lambda e, kk=k: e.tensor_copy(
                    he[:, kk:kk + 1], scv[:, kk * TT + TT - 1:kk * TT + TT]),
                  reads=screg, writes=[("he", k)], tag=f"t{tj}_hecp{k}")

        def emit_mid(ti):
            CUR["tile"] = ti
            use_stats(ti % 2)
            ax = ax0b[ti % 2]
            xnm = f"x0{ti % 2}"

            def yact_evac(bk_, m, _):
                A("act", lambda e, b=pb[bk_], mm=m: e.activation(
                    chk(au, mm), b[:], AF.Silu, bias=byact_s[:, mm:mm + 1]),
                  reads=[("pb", bk_), "byact"], writes=[("u", m)], tag=f"t{ti}_sy_{m}")

            mm_phase("wyact", KC, KC, lambda k: ("ba", k),
                     lambda k: chk(ba, k), yact_evac, f"t{ti}_yact")

            def y_evac(bk_, m, _):
                tp, treg = tmp_slot()
                A("dve", lambda e, b=pb[bk_], mm=m, d=tp: e.scalar_tensor_tensor(
                    d[:], b[:], by_s[:, mm:mm + 1], chk(au, mm), op0=OP.add, op1=OP.mult),
                  reads=[("pb", bk_), "by", ("u", m)], writes=[treg], tag=f"t{ti}_y_{m}")
                A("dve", lambda e, mm=m, s=tp: e.tensor_tensor(
                    chk(at1, mm), chk(at1, mm), s[:], op=OP.add),
                  reads=[("t1", m), treg], writes=[("t1", m)], tag=f"t{ti}_xp_{m}")

            mm_phase("wy", KC, KC, lambda k: ("bb2", k),
                     lambda k: bbig2[:, k * TT:(k + 1) * TT], y_evac, f"t{ti}_y")
            if ti == 0:
                tap("xp", lambda: at1[:], arena_regions("t1"), [P, T])

            stats(at1, "t1", False, EPS_RMS, f"t{ti}_rms2")
            rms_apply(at1, "t1", gffn_s, "gffn", f"t{ti}_rms2")

            def fc_evac(bfc_bank, m, bfca_bank):
                tp, treg = tmp_slot()
                A("act", lambda e, b=pb[bfca_bank], mm=m, d=tp: e.activation(
                    d[:], b[:], AF.Silu, bias=bfca_s[:, mm:mm + 1]),
                  reads=[("pb", bfca_bank), "bfca"], writes=[treg], tag=f"t{ti}_fs_{m}")
                A("dve", lambda e, b=pb[bfc_bank], mm=m, s=tp: e.scalar_tensor_tensor(
                    bbig1[:, mm * TT:(mm + 1) * TT], b[:], bfc_s[:, mm:mm + 1], s[:],
                    op0=OP.add, op1=OP.mult),
                  reads=[("pb", bfc_bank), "bfc", treg], writes=[("bb1", m)],
                  tag=f"t{ti}_hid_{m}")

            mm_phase("wfc", FC, KC, lambda k: ("ba", k),
                     lambda k: chk(ba, k), fc_evac, f"t{ti}_fc",
                     extra_weight=("wfca", None))
            if ti == 0:
                tap("hid", lambda: bbig1[:], [("bb1", c) for c in range(FC)],
                    [P, 4 * T], BF16)

        def emit_mid2(ti):
            CUR["tile"] = ti
            use_stats(ti % 2)
            ax = ax0b[ti % 2]
            xnm = f"x0{ti % 2}"

            def out_evac(bk_, m, _):
                A("dve", lambda e, b=pb[bk_], mm=m: e.scalar_tensor_tensor(
                    chk(au, mm), b[:], bout_s[:, mm:mm + 1], chk(at1, mm),
                    op0=OP.add, op1=OP.add),
                  reads=[("pb", bk_), "bout", ("t1", m)], writes=[("u", m)],
                  tag=f"t{ti}_ao_{m}")

            mm_phase("wout", KC, FC, lambda k: ("bb1", k),
                     lambda k: bbig1[:, k * TT:(k + 1) * TT], out_evac, f"t{ti}_out")

            if ti == 0:
                tap("ao", lambda: au[:], arena_regions("u"), [P, T])

            stats(au, "u", True, EPS_LN, f"t{ti}_ln2")
            ln_apply(au, "u", au, "u", f"t{ti}_ln2", resid=ax, resid_name=xnm)

            rb_ = banks(1)[0]
            for k in range(KC):
                A("pe", lambda e, b=pb[rb_], kk=k, axx=ax, st=(k == 0), sp=(k == KC - 1):
                    e.matmul(b[0:4, :],
                             wrt_s[:, kk * E:(kk + 1) * E], chk(axx, kk),
                             start=st, stop=sp),
                  reads=["wrt", (xnm, k)], writes=[("pb", rb_)], tag=f"t{ti}_rt{k}")
            _, b1_ln3 = stats(ax, xnm, True, EPS_LN, f"t{ti}_ln3")
            A("dve", lambda e, b=pb[rb_]: e.tensor_copy(l4[:], b[0:4, :]),
              reads=[("pb", rb_)], writes=["l4"], tag=f"t{ti}_l4")
            if ti == 0:
                tap("x1t", lambda axx=ax: axx[:], arena_regions(xnm), [P, T])
                tap("l4raw", lambda: l4[:], ["l4"], [4, TT])
            A("dve", lambda e, s=pb[b1_ln3]: e.scalar_tensor_tensor(
                l4[:], s[0:4, :], negsRD[:, 0:1], l4[:], op0=OP.mult, op1=OP.add),
              reads=[("pb", b1_ln3), "negsRD", "l4"], writes=["l4"],
              tag=f"t{ti}_lsh")
            bx = banks(3)
            for idx in range(3):
                A("pe", lambda e, b=pb[bx[idx]], jj=idx + 1: e.matmul(
                    b[0:1, :], exb[0:4, jj:jj + 1], l4[0:4, :],
                    start=True, stop=True),
                  reads=["exb0", "l4"], writes=[("pb", bx[idx])],
                  tag=f"t{ti}_ex{idx}")
            A("dve", lambda e, b=pb[bx[1]]: e.tensor_copy(lrows[:], b[0:1, :]),
              reads=[("pb", bx[1])], writes=[("lr", 0)], tag=f"t{ti}_l2")
            R = lambda i: rwk[:, i * TT:(i + 1) * TT]
            A("dve", lambda e, b=pb[bx[0]]: e.tensor_tensor(
                R(0), l4[0:1, :], b[0:1, :], op=OP.max),
              reads=["l4", ("pb", bx[0])], writes=[("rw", 0)], tag=f"t{ti}_mx1")
            A("dve", lambda e, b=pb[bx[2]]: e.tensor_tensor(
                R(3), lrows[:], b[0:1, :], op=OP.max),
              reads=[("lr", 0), ("pb", bx[2])], writes=[("rw", 3)], tag=f"t{ti}_mx2")
            A("dve", lambda e: e.tensor_tensor(R(0), R(0), R(3), op=OP.max),
              reads=[("rw", 0), ("rw", 3)], writes=[("rw", 0)], tag=f"t{ti}_mx3")

            mbk = banks(4)
            for ee_ in range(E):
                if ee_ == 0:
                    A("dve", lambda e: e.tensor_tensor(R(1), l4[0:1, :], R(0), op=OP.is_equal),
                      reads=["l4", ("rw", 0)], writes=[("rw", 1)], tag=f"t{ti}_m0")
                    A("dve", lambda e: e.tensor_scalar(R(2), R(1), -1.0, 1.0,
                                                       op0=OP.mult, op1=OP.add),
                      reads=[("rw", 1)], writes=[("rw", 2)], tag=f"t{ti}_nm0")
                elif ee_ < 3:
                    lsrc = ((lambda e, b=pb[bx[0]]: e.tensor_tensor(
                                R(3), b[0:1, :], R(0), op=OP.is_equal))
                            if ee_ == 1 else
                            (lambda e: e.tensor_tensor(
                                R(3), lrows[:], R(0), op=OP.is_equal)))
                    lreg = ("pb", bx[0]) if ee_ == 1 else ("lr", 0)
                    A("dve", lsrc, reads=[lreg, ("rw", 0)], writes=[("rw", 3)],
                      tag=f"t{ti}_eq{ee_}")
                    A("dve", lambda e: e.tensor_tensor(R(1), R(3), R(2), op=OP.mult),
                      reads=[("rw", 3), ("rw", 2)], writes=[("rw", 1)], tag=f"t{ti}_m{ee_}")
                    A("dve", lambda e: e.tensor_scalar(R(3), R(1), -1.0, 1.0,
                                                       op0=OP.mult, op1=OP.add),
                      reads=[("rw", 1)], writes=[("rw", 3)], tag=f"t{ti}_t{ee_}")
                    A("dve", lambda e: e.tensor_tensor(R(2), R(2), R(3), op=OP.mult),
                      reads=[("rw", 2), ("rw", 3)], writes=[("rw", 2)], tag=f"t{ti}_nm{ee_}")
                else:
                    A("dve", lambda e: e.tensor_copy(R(1), R(2)),
                      reads=[("rw", 2)], writes=[("rw", 1)], tag=f"t{ti}_m3")
                A("dve", lambda e, ex=ee_: e.tensor_copy(mbf[ex], R(1)),
                  reads=[("rw", 1)], writes=[("bb2", 4 + ee_)], tag=f"t{ti}_mb{ee_}")
                DMA("qact", lambda e, ex=ee_: e.dma_start(mrows[ex:ex + 1, :], mbf[ex]),
                    reads=[("bb2", 4 + ee_)], writes=[("mr", ee_)], tag=f"t{ti}_mrd{ee_}")
                A("pe", lambda e, ex=ee_, b=pb[mbk[ee_]]: e.matmul(
                    b[:], obf[:], mbf[ex], start=True, stop=True),
                  reads=[("bb2", 4 + ee_), "obf"], writes=[("pb", mbk[ee_])],
                  tag=f"t{ti}_mbc{ee_}")

            ln_apply(ax, xnm, aal, "al", f"t{ti}_ln3")
            if ti == 0:
                tap("t2", lambda: aal[:], arena_regions("al"), [P, T])
            for ee_ in range(E):
                A("dve", lambda e, ex=ee_, b=pb[mbk[ee_]]: e.tensor_copy(
                    mbb[ex], b[:]),
                  reads=[("pb", mbk[ee_])], writes=[("bb2", ee_)], tag=f"t{ti}_mbe{ee_}")

            for ee_ in range(E):
                A("dve", lambda e, ex=ee_: e.tensor_tensor(
                    bbig1[:, ex * T:(ex + 1) * T].rearrange("p (c t) -> p c t", c=KC),
                    aal[:].rearrange("p (c t) -> p c t", c=KC),
                    mbb[ex].unsqueeze(1).broadcast_to([P, KC, TT]), op=OP.mult),
                  reads=arena_regions("al") + [("bb2", ee_)],
                  writes=[("bb1", ee_ * KC + k) for k in range(KC)], tag=f"t{ti}_xm{ee_}")

            if ti == 0:
                tap("l4", lambda: l4[:], ["l4"], [4, TT])
                tap("mr", lambda: mrows[:], [("mr", x) for x in range(E)], [4, TT], BF16)
                tap("xm", lambda: bbig1[:], [("bb1", c) for c in range(FC)],
                    [P, 4 * T], BF16)

            n_half1 = FC // 2
            for h in range(n_half1):
                bk = banks(2)
                slots = []
                for ee_ in range(E):
                    for kg in range(2):
                        slots.append(load_block("w1", h, kg, extra_key=ee_))
                for ee_ in range(E):
                    for k in range(KC):
                        kg, kk = divmod(k, 4)
                        s1, r1 = slots[ee_ * 2 + kg]
                        for j in range(2):
                            A("pe", lambda e, b=pb[bk[j]], w=lhsT(s1, kk, j),
                                r=bbig1[:, ee_ * T + k * TT: ee_ * T + (k + 1) * TT],
                                st=(ee_ == 0 and k == 0), sp=False:
                                e.matmul(b[:], w, r, start=st, stop=sp),
                              reads=[r1, ("bb1", ee_ * KC + k)], writes=[("pb", bk[j])],
                              tag=f"t{ti}_w1_h{h}e{ee_}k{k}j{j}")
                for j in range(2):
                    c = 2 * h + j
                    A("pe", lambda e, b=pb[bk[j]], cc=c: e.matmul(
                        b[:], b1all[:, cc * P:(cc + 1) * P], mrows[:],
                        start=False, stop=True),
                      reads=["b1all"] + [("mr", x) for x in range(E)],
                      writes=[("pb", bk[j])], tag=f"t{ti}_w1b_h{h}j{j}")
                for j in range(2):
                    c = 2 * h + j
                    A("act", lambda e, b=pb[bk[j]], cc=c: e.activation(
                        bbig2[:, cc * TT:(cc + 1) * TT], b[:], AF.Gelu),
                      reads=[("pb", bk[j])], writes=[("bb2", c)], tag=f"t{ti}_gel{c}")

            def wo_evac(bk_, m, _):
                A("dve", lambda e, b=pb[bk_], mm=m: e.scalar_tensor_tensor(
                    chk(aal, mm), b[:], bo_s[:, mm:mm + 1], chk(aal, mm),
                    op0=OP.add, op1=OP.add),
                  reads=[("pb", bk_), "bo", ("al", m)], writes=[("al", m)],
                  tag=f"t{ti}_yf_{m}")

            mm_phase("wo", KC, FC, lambda k: ("bb2", k),
                     lambda k: bbig2[:, k * TT:(k + 1) * TT], wo_evac, f"t{ti}_wo")

        def emit_tail(ti):
            CUR["tile"] = ti
            use_stats(ti % 2)
            ax = ax0b[ti % 2]
            xnm = f"x0{ti % 2}"
            stats(aal, "al", True, EPS_IN, f"t{ti}_lni", fixed=LNIB)
            ln_apply(aal, "al", aal, "al", f"t{ti}_lni")
            for k in range(KC):
                A("act", lambda e, kk=k: e.activation(
                    chk(au, kk), chk(aal, kk), AF.Identity,
                    bias=bet_s[:, kk:kk + 1], scale=gam_s[:, kk:kk + 1]),
                  reads=[("al", k), "gam", "bet"], writes=[("u", k)],
                  tag=f"t{ti}_gb{k}")

            stats(au, "u", True, EPS_LN, f"t{ti}_ln4", fixed=LN4B)
            ln_apply(au, "u", au, "u", f"t{ti}_ln4", resid=ax, resid_name=xnm)

            dst = out_t.rearrange("(k p) t -> p k t", p=P)[
                :, :, ti * TT:(ti + 1) * TT]
            DMA("qact", lambda e, d=dst, axx=ax: e.dma_start(d, c3(axx)),
                reads=arena_regions(xnm), tag=f"t{ti}_st")

        x0_load(0)
        x0_load(1)
        emit_front_a1(0)
        hs1 = emit_front_a2(0)
        emit_front_b(0, hs1)
        for ti in range(NT):
            emit_mid(ti)
            if ti + 1 < NT:
                emit_front_a1(ti + 1)
            emit_mid2(ti)
            hs1 = emit_front_a2(ti + 1) if ti + 1 < NT else None
            emit_tail(ti)
            if hs1 is not None:
                emit_front_b(ti + 1, hs1)

        nc._sbuf_rem = nc.sbuf_bytes_remaining
        sched.finalize()
        blk = stack.enter_context(nc.Block())

        @blk.tensor
        def _(eng):
            sched.emit_stream("pe", eng, esems, qsems)

        @blk.scalar
        def _(eng):
            sched.emit_stream("act", eng, esems, qsems)

        @blk.vector
        def _(eng):
            sched.emit_stream("dve", eng, esems, qsems)

        @blk.sync
        def _(eng):
            sched.emit_stream("sp", eng, esems, qsems)

        @blk.gpsimd
        def _(eng):
            sched.emit_stream("gp", eng, esems, qsems)

    nc._sched = sched
    return nc


def _host_layout(inputs):
    x = np.asarray(inputs["L0_attn_x"])
    def wT(a):
        return np.ascontiguousarray(np.asarray(a).T)
    def vchunk(v, n):
        return np.ascontiguousarray(np.asarray(v).reshape(n, P).T)

    common = {
        "wlnz": wT(inputs["L0_attn_w_ln_z"]),
        "wdt": wT(inputs["L0_attn_w_dt"]),
        "wy": wT(inputs["L0_attn_w_y"]),
        "wyact": wT(inputs["L0_attn_w_yact"]),
        "wfc": wT(inputs["L0_attn_w_fc"]),
        "wfca": wT(inputs["L0_attn_w_fcact"]),
        "wout": wT(inputs["L0_attn_w_out"]),
        "w1": np.ascontiguousarray(np.asarray(inputs["L0_ffn_W1"])),
        "wo": np.ascontiguousarray(np.asarray(inputs["L0_ffn_Wo"])),
        "wrt": np.ascontiguousarray(
            np.asarray(inputs["L0_ffn_W_router"]).reshape(KC, P, E).transpose(1, 0, 2)),
        "b1": np.ascontiguousarray(np.asarray(inputs["L0_ffn_b1"])),
        "blnz": vchunk(inputs["L0_attn_b_ln_z"], KC),
        "bdt": vchunk(inputs["L0_attn_b_dt"], KC),
        "by": vchunk(inputs["L0_attn_b_y"], KC),
        "byact": vchunk(inputs["L0_attn_b_yact"], KC),
        "bout": vchunk(inputs["L0_attn_b_out"], KC),
        "bo": vchunk(inputs["L0_ffn_bo"], KC),
        "gam": vchunk(inputs["L0_ffn_ln_gamma"], KC),
        "bet": vchunk(inputs["L0_ffn_ln_beta"], KC),
        "gsio": vchunk(inputs["L0_attn_g_sio"], KC),
        "gffn": vchunk(inputs["L0_attn_g_ffn"], KC),
        "h0": vchunk(inputs["L0_attn_h0"], KC),
        "bfc": vchunk(inputs["L0_attn_b_fc"], FC),
        "bfca": vchunk(inputs["L0_attn_b_fcact"], FC),
    }
    in_maps = []
    for b in range(8):
        m = dict(common)
        m["xt"] = np.ascontiguousarray(x[b].T)
        in_maps.append(m)
    return in_maps


def kernel(**inputs):
    in_maps = _host_layout(inputs)
    nc = build_nc()
    res = run_bass_kernel_spmd(nc, in_maps, core_ids=list(range(8)))
    outs = []
    for b in range(8):
        o = np.asarray(res.results[b]["out"], dtype=np.float32)
        outs.append(o.T)
    return np.ascontiguousarray(np.stack(outs, axis=0))
